# version 08 (final)
# baseline (speedup 1.0000x reference)
import numpy as np
import ml_dtypes
import concourse.bass as bass
import concourse.mybir as mybir
from concourse.bass_utils import run_bass_kernel_spmd
from contextlib import ExitStack

F32 = mybir.dt.float32
BF16 = mybir.dt.bfloat16
AF = mybir.ActivationFunctionType
ALU = mybir.AluOpType
AX = mybir.AxisListType

B, T, C = 2, 4096, 2048
HPC = 4
TOK = 1024
NE, INTER = 8, 2048
SCALE = float(128) ** -0.5
EPS = 1e-5
NG_OWN = 2
NG_FULL = 8
GROUPS = [[0, 1, 2, 3], [4, 5, 6, 7]]


class Ctr:
    def __init__(self, stack, nc, name):
        self.h = stack.enter_context(nc.semaphore(name))
        self.n = 0

    def inc(self, inst, k=1):
        inst.then_inc(self.h, k)
        self.n += k
        return inst

    def wait(self, eng, val=None):
        v = self.n if val is None else val
        if v > 0:
            eng.wait_ge(self.h, v)


class _NS:
    pass


class CtrP:

    def __init__(self, stack, nc, name):
        self.c = [Ctr(stack, nc, name + "_0"), Ctr(stack, nc, name + "_1")]
        self.nd = 0

    def inc(self, inst):
        self.c[self.nd % 2].inc(inst, 16)
        self.nd += 1

    def wait_done(self, eng, idx):
        eng.wait_ge(self.c[idx % 2].h, 16 * (idx // 2 + 1))

    def wait_all(self, eng):
        for c in self.c:
            c.wait(eng)


def _dbuf(stack, nc, name, n, shape, dt, psum=False):
    f = nc.psum_tensor if psum else nc.sbuf_tensor
    return [stack.enter_context(f("%s_%d" % (name, i), shape, dt)) for i in range(n)]


def build_program(upto=99, pa_mode=9, debug=False):
    nc = bass.Bass("TRN2", target_bir_lowering=False)
    st = ExitStack()
    z = _NS()
    z.nc = nc

    din = lambda n, s, d: nc.dram_tensor(n, s, d, kind="ExternalInput")
    z.x_cat = din("x_cat", [TOK + T, C], F32)
    z.wq = din("wq", [C, HPC * 128], F32)
    z.wk = din("wk", [C, HPC * 128], F32)
    z.wv = din("wv", [C, HPC * 128], F32)
    z.bq = din("bq", [HPC * 128], F32)
    z.bk = din("bk", [HPC * 128], F32)
    z.bv_row = din("bv_row", [1, HPC * 128], F32)
    z.wo = din("wo", [C, C], F32)
    z.gate_w = din("gate_w", [C, NE], F32)
    z.fc1w = din("fc1w", [NE, C, INTER], F32)
    z.fc1bt = din("fc1bt", [INTER, NE], F32)
    z.fc2w = din("fc2w", [NE, INTER, C], F32)
    z.fc2be = din("fc2be", [NE + 1, C], F32)
    z.alpha_row = din("alpha_row", [1, NE], F32)
    z.msel_in = din("msel", [128, 4], F32)
    z.idb_in = din("idb", [128, 128], F32)
    z.idf_in = din("idf", [128, 128], F32)
    z.ones_col_in = din("ones_col", [128, 1], F32)
    z.ones_rowf_in = din("ones_rowf", [1, 128], F32)
    z.ones_rowb_in = din("ones_rowb", [1, 512], F32)
    z.sel8_in = din("sel8", [NE, NE * 128], F32)
    z.cmask_in = din("cmask", [128, 4 * 512], F32)
    z.out_own = nc.dram_tensor("out_own", [TOK, C], F32, kind="ExternalOutput")
    z.debug = debug
    if debug:
        z.dbg_comb = nc.dram_tensor("dbg_comb", [128, NG_OWN * 4, NE], F32,
                                    kind="ExternalOutput")
        z.dbg_combT = nc.dram_tensor("dbg_combT", [NE + 1, TOK], BF16,
                                     kind="ExternalOutput")
        z.dbg_gstat = nc.dram_tensor("dbg_gstat", [128, 96], F32,
                                     kind="ExternalOutput")

    z.wq_bf = nc.dram_tensor("wq_bf", [16, 128, HPC * 128], BF16, kind="Internal")
    z.wk_bf = nc.dram_tensor("wk_bf", [16, 128, HPC * 128], BF16, kind="Internal")
    z.wv_bf = nc.dram_tensor("wv_bf", [16, 128, HPC * 128], BF16, kind="Internal")
    z.wo_bf = nc.dram_tensor("wo_bf", [16, 128, C], BF16, kind="Internal")
    z.xnod = nc.dram_tensor("xnod", [16, 128, TOK], BF16,
                            kind="ExternalOutput" if debug else "Internal")
    z.ag_in = nc.dram_tensor("ag_in", [HPC * 128, T], BF16, kind="Internal")
    z.ag_out = nc.dram_tensor("ag_out", [C, T], BF16, kind="Internal")
    z.attn_d = nc.dram_tensor("attn_d", [TOK, C], F32, kind="Internal")

    z.SY, z.TE, z.VE, z.SC, z.GP = nc.sync, nc.tensor, nc.vector, nc.scalar, nc.gpsimd
    ALL_ENG = (z.SY, z.TE, z.VE, z.SC, z.GP)

    sbp = lambda name, shape, dt: st.enter_context(nc.sbuf_tensor(name, shape, dt))
    z.idb = sbp("s_idb", [128, 128], BF16)
    z.idf = sbp("s_idf", [128, 128], F32)
    z.ones_col = sbp("s_onescol", [128, 1], BF16)
    z.ones_rowf = sbp("s_onesrowf", [1, 128], F32)
    z.ones_rowb = sbp("s_onesrowb", [1, 512], BF16)
    z.msel = sbp("s_msel", [128, 4], F32)
    z.bqs = sbp("s_bqs", [128, HPC], F32)
    z.bkT = sbp("s_bkT", [128, HPC], F32)
    z.alpha_bc = sbp("s_alphabc", [128, NE], F32)
    z.bv_bc = sbp("s_bvbc", [128, HPC * 128], F32)
    z.gw_b = sbp("s_gwb", [128, 16, NE], BF16)
    z.f1bt = sbp("s_f1bt", [128, 16 * NE], F32)
    z.f2b_b = sbp("s_f2bb", [NE + 1, C], BF16)
    z.comb = sbp("s_comb", [128, NG_OWN * 4, NE], F32)
    z.combT = sbp("s_combT", [NE + 1, TOK], BF16)
    z.gstat = sbp("s_gstat", [128, 96], F32)
    z.eps_t = sbp("s_epst", [128, 1], F32)

    mkc = lambda name: Ctr(st, nc, name)
    z.mkc = mkc
    z.st = st
    c_bar = mkc("c_bar")

    def barrier():
        for e in ALL_ENG:
            if e is not z.SY:
                e.drain()
            e.sem_inc(c_bar.h, 1)
        c_bar.n += 5
        for e in ALL_ENG:
            e.wait_ge(c_bar.h, c_bar.n)

    z.barrier = barrier
    z.upto = upto
    z.pa_mode = pa_mode
    block = st.enter_context(nc.Block())
    block.sync(lambda _e: _emit_all(z, st))
    st.close()
    return nc


def _dummy_out(z):
    nc, SY = z.nc, z.SY
    with nc.sbuf_tensor("s_dummy", [128, C], F32) as dmy:
        c_d = z.mkc("c_dmy")
        z.VE.memset(dmy[:], 0.0)
        z.VE.sem_inc(c_d.h, 1)
        c_d.n += 1
        c_d.wait(SY)
        c_o = z.mkc("c_dmyo")
        for tc in range(8):
            c_o.inc(SY.dma_start(out=z.out_own.ap()[tc * 128:(tc + 1) * 128, :],
                                 in_=dmy[:]), 16)
        SY.wait_ge(c_o.h, 128)


def _emit_all(z, st):
    _p0(z)
    z.barrier()
    if z.upto <= 1:
        _dummy_out(z)
        return
    s_ab = ExitStack()
    s_pa = ExitStack()
    _alloc_pab(z, s_ab, s_pa)
    _pa(z)
    z.barrier()
    if z.debug:
        c_dbg = z.mkc("c_dbg")
        c_dbg.inc(z.SY.dma_start(out=z.dbg_comb.ap(), in_=z.comb[:]), 16)
        c_dbg.inc(z.SY.dma_start(out=z.dbg_combT.ap(), in_=z.combT[:]), 16)
        c_dbg.inc(z.SY.dma_start(out=z.dbg_gstat.ap(), in_=z.gstat[:]), 16)
        z.SY.wait_ge(c_dbg.h, 48)
        z.barrier()
    s_pa.close()
    if z.upto <= 2:
        s_ab.close()
        _dummy_out(z)
        return
    _pb(z)
    z.barrier()
    s_ab.close()
    if z.upto <= 3:
        _dummy_out(z)
        return
    z.ffn_acc = st.enter_context(z.nc.sbuf_tensor("s_facc", [128, 16, TOK], F32))
    s_c = ExitStack()
    _pc(z, s_c)
    z.barrier()
    s_c.close()
    if z.upto <= 4:
        _dummy_out(z)
        return
    s_d = ExitStack()
    _pd(z, s_d)
    z.barrier()
    s_d.close()
    if z.upto <= 5:
        _dummy_out(z)
        return
    _pe2(z, st)


def st_of(z):
    return z.st


def _p0(z):
    nc, SY, TE, VE, SC, GP = z.nc, z.SY, z.TE, z.VE, z.SC, z.GP
    mkc = z.mkc
    s = ExitStack()
    sb = lambda name, shape, dt: s.enter_context(nc.sbuf_tensor(name, shape, dt))
    bqT = sb("s_bqT", [128, HPC], F32)
    idb_f = sb("s_idbf", [128, 128], F32)
    onec_f = sb("s_onecf", [128, 1], F32)
    onerb_f = sb("s_onerbf", [1, 512], F32)
    alpha_r = sb("s_alphar", [1, NE], F32)
    bvr = sb("s_bvr", [1, HPC * 128], F32)
    gw_f = sb("s_gwf", [128, 16, NE], F32)
    f2b_f = sb("s_f2bf", [NE + 1, C], F32)
    alpha_b = sb("s_alphab", [1, NE], BF16)
    bvr_b = sb("s_bvrb", [1, HPC * 128], BF16)
    wstg = _dbuf(s, nc, "s_wstg", 2, [128, C], F32)
    wstgb = _dbuf(s, nc, "s_wstgb", 2, [128, C], BF16)
    p_bc = s.enter_context(nc.psum_tensor("p_bc", [128, 512], F32))

    c_ld = mkc("c_ld")
    for dst, src in (
        (idb_f, z.idb_in), (z.idf, z.idf_in), (onec_f, z.ones_col_in),
        (z.ones_rowf, z.ones_rowf_in), (onerb_f, z.ones_rowb_in),
        (z.msel, z.msel_in), (alpha_r, z.alpha_row), (bvr, z.bv_row),
    ):
        c_ld.inc(SY.dma_start(out=dst[:], in_=src.ap()[:]), 16)
    for h in range(HPC):
        c_ld.inc(SY.dma_start(
            out=bqT[:, h:h + 1],
            in_=z.bq.ap()[h * 128:(h + 1) * 128].rearrange("(p o) -> p o", o=1)), 16)
        c_ld.inc(SY.dma_start(
            out=z.bkT[:, h:h + 1],
            in_=z.bk.ap()[h * 128:(h + 1) * 128].rearrange("(p o) -> p o", o=1)), 16)
    for cc in range(16):
        c_ld.inc(SY.dma_start(
            out=gw_f[:, cc, :], in_=z.gate_w.ap()[cc * 128:(cc + 1) * 128, :]), 16)
        c_ld.inc(SY.dma_start(
            out=z.f1bt[:, cc * NE:(cc + 1) * NE],
            in_=z.fc1bt.ap()[cc * 128:(cc + 1) * 128, :]), 16)
    c_ld.inc(SY.dma_start(out=f2b_f[:], in_=z.fc2be.ap()[:]), 16)

    c_w0 = mkc("c_w0")
    c_ld.wait(VE)
    VE.memset(z.eps_t[:], EPS)
    VE.tensor_copy(z.idb[:], idb_f[:])
    VE.tensor_copy(z.ones_col[:], onec_f[:])
    VE.tensor_copy(z.ones_rowb[:], onerb_f[:])
    VE.tensor_scalar_mul(z.bqs[:], bqT[:], SCALE)
    VE.tensor_copy(z.gw_b[:], gw_f[:])
    c_w0.inc(VE.tensor_copy(z.f2b_b[:], f2b_f[:]))
    c_bc = mkc("c_bc")
    c_ld.wait(VE)
    VE.tensor_copy(alpha_b[:], alpha_r[:])
    c_w0.inc(VE.tensor_copy(bvr_b[:], bvr[:]))
    c_w0.wait(TE)
    c_bc.inc(TE.matmul(p_bc[:, 0:NE], z.ones_rowb[:, 0:128], alpha_b[:],
                       start=True, stop=True))
    c_bc.wait(VE)
    c_w0.inc(VE.tensor_copy(z.alpha_bc[:], p_bc[:, 0:NE]))
    c_w0.wait(TE)
    c_bc.inc(TE.matmul(p_bc[:], z.ones_rowb[:, 0:128], bvr_b[:], start=True, stop=True))
    c_bc.wait(VE)
    c_w0.inc(VE.tensor_copy(z.bv_bc[:], p_bc[:]))

    c_wi = CtrP(st_of(z), nc, "c_wi")
    c_wc = mkc("c_wc")
    c_wo2 = CtrP(st_of(z), nc, "c_wo2")
    jobs = ([(z.wq, z.wq_bf, 512, cc) for cc in range(16)]
            + [(z.wk, z.wk_bf, 512, cc) for cc in range(16)]
            + [(z.wv, z.wv_bf, 512, cc) for cc in range(16)]
            + [(z.wo, z.wo_bf, 2048, cc) for cc in range(16)])
    for ji, (src, dst, ncol, cc) in enumerate(jobs):
        p = ji % 2
        if ji >= 2:
            SY.wait_ge(c_wc.h, ji - 1)
        c_wi.inc(SY.dma_start(out=wstg[p][:, 0:ncol],
                              in_=src.ap()[cc * 128:(cc + 1) * 128, :]))
        c_wi.wait_done(GP, ji)
        if ji >= 2:
            c_wo2.wait_done(GP, ji - 2)
        c_wc.inc(GP.tensor_copy(wstgb[p][:, 0:ncol], wstg[p][:, 0:ncol]))
        SC.wait_ge(c_wc.h, ji + 1)
        c_wo2.inc(SC.dma_start(out=dst.ap()[cc][:, 0:ncol], in_=wstgb[p][:, 0:ncol]))
    c_wo2.wait_all(SY)
    s.close()


def _alloc_pab(z, s, s_pa):
    nc = z.nc
    sb = lambda name, shape, dt: s.enter_context(nc.sbuf_tensor(name, shape, dt))
    z.QT = sb("s_QT", [128, HPC, T], BF16)
    z.KT = sb("s_KT", [128, HPC, T], BF16)
    z.Vsb = sb("s_V", [128, HPC, 32, 128], BF16)
    z.yT = sb("s_yT", [128, HPC, T], BF16)
    sp = lambda name, shape, dt: s_pa.enter_context(nc.sbuf_tensor(name, shape, dt))
    z.xbuf = _dbuf(s_pa, nc, "s_xbuf", 2, [128, C], F32)
    z.dump = sp("s_dump", [128, C], BF16)
    z.xnb = _dbuf(s_pa, nc, "s_xnb", 2, [128, C], BF16)
    z.lnst = _dbuf(s_pa, nc, "s_lnst", 2, [128, 8], F32)
    z.xg = _dbuf(s_pa, nc, "s_xg", 2, [128, 16, 512], BF16)
    z.wqs = _dbuf(s_pa, nc, "s_wqs", 2, [128, HPC * 128], BF16)
    z.wks = _dbuf(s_pa, nc, "s_wks", 2, [128, HPC * 128], BF16)
    z.wvs = _dbuf(s_pa, nc, "s_wvs", 2, [128, HPC * 128], BF16)
    z.comb32 = s_pa.enter_context(nc.sbuf_tensor("s_comb32", [128, 128], BF16))
    z.p_tr = _dbuf(s_pa, nc, "p_tr", 2, [128, 512], F32, psum=True)


def _pa(z):
    nc, SY, TE, VE, SC, GP = z.nc, z.SY, z.TE, z.VE, z.SC, z.GP
    mkc = z.mkc
    xbuf, dump, xnb, lnst, xg = z.xbuf, z.dump, z.xnb, z.lnst, z.xg
    QT, KT, Vsb = z.QT, z.KT, z.Vsb
    wqs, wks, wvs, p_tr = z.wqs, z.wks, z.wvs, z.p_tr

    z.c_xod = mkc("c_xod")
    z.c_gct = mkc("c_gct")
    c_x = CtrP(st_of(z), nc, "c_x")
    c_sx = mkc("c_sx")
    c_ve = mkc("c_ve")
    c_rs = mkc("c_rs")
    c_xn = mkc("c_xn")
    c_tp = mkc("c_tp")
    c_tpe = mkc("c_tpe")
    c_gmm = mkc("c_gmm")
    c_ctm = mkc("c_ctm")
    c_gv = mkc("c_gv")
    c_ge = mkc("c_ge")
    c_gc = mkc("c_gc")
    z.c_gv2 = mkc("c_gv2")
    z.c_gs2 = mkc("c_gs2")
    z.pg_free = {}
    c_wql = CtrP(st_of(z), nc, "c_wql")
    c_wqf = mkc("c_wqf")
    c_wvl = CtrP(st_of(z), nc, "c_wvl")
    c_wvf = mkc("c_wvf")
    c_qev = mkc("c_qev")
    c_vev = mkc("c_vev")

    VE.memset(z.comb32[:], 0.0)

    def ln_tile(t):
        p = t % 2
        if t >= 2:
            SY.wait_ge(c_xn.h, t - 1)
        c_x.inc(SY.dma_start(out=xbuf[p][:], in_=z.x_cat.ap()[t * 128:(t + 1) * 128, :]))
        c_x.wait_done(SC, t)
        if t >= 2:
            SC.wait_ge(c_ve.h, t - 1)
        c_sx.inc(SC.activation(dump[:], xbuf[p][:], AF.Copy, accum_out=lnst[p][:, 0:1]))
        VE.wait_ge(c_sx.h, t + 1)
        VE.scalar_tensor_tensor(dump[:], xbuf[p][:], 1.0, xbuf[p][:], ALU.mult, ALU.mult,
                                accum_out=lnst[p][:, 1:2])
        VE.tensor_scalar_mul(lnst[p][:, 2:3], lnst[p][:, 0:1], -1.0 / C)
        VE.tensor_mul(lnst[p][:, 3:4], lnst[p][:, 2:3], lnst[p][:, 2:3])
        c_ve.inc(VE.scalar_tensor_tensor(lnst[p][:, 4:5], lnst[p][:, 1:2], 1.0 / C,
                                         lnst[p][:, 3:4], ALU.mult, ALU.subtract))
        SC.wait_ge(c_ve.h, t + 1)
        SC.activation(lnst[p][:, 5:6], lnst[p][:, 4:5], AF.Ln, bias=z.eps_t[:])
        c_rs.inc(SC.activation(lnst[p][:, 6:7], lnst[p][:, 5:6], AF.Exp, scale=-0.5))
        VE.wait_ge(c_rs.h, t + 1)
        if t >= 2:
            VE.wait_ge(c_tp.h, 4 * (t - 1))
        c_xn.inc(VE.tensor_scalar(xnb[p][:], xbuf[p][:], lnst[p][:, 2:3], lnst[p][:, 6:7],
                                  ALU.add, ALU.mult))

    def transpose_tile(t):
        grp, tl = t // 4, t % 4
        gp = grp % 2
        p = t % 2
        for q in range(4):
            gi = t * 4 + q
            TE.wait_ge(c_xn.h, t + 1)
            if gi >= 2:
                TE.wait_ge(c_tpe.h, gi - 1)
            for j in range(4):
                ccol = q * 4 + j
                inst = TE.matmul(p_tr[gi % 2][:, j * 128:(j + 1) * 128],
                                 xnb[p][:, ccol * 128:(ccol + 1) * 128], z.idb[:],
                                 start=True, stop=True)
            c_tp.inc(inst)
            VE.wait_ge(c_tp.h, gi + 1)
            c_tpe.inc(VE.tensor_copy(
                xg[gp][:, q * 4:(q + 1) * 4, tl * 128:(tl + 1) * 128],
                p_tr[gi % 2][:].rearrange("p (c t) -> p c t", c=4)))

    def gating_group(og):
        gp = og % 2
        pm_eff = z.pa_mode
        if 20 <= z.pa_mode < 30:
            pm_eff = (z.pa_mode - 20) if og == 0 else 4
        if z.pa_mode >= 30:
            pm_eff = 6 if og == 0 else 4
        wait_only = (z.pa_mode == 40)
        if wait_only:
            pm_eff = 6 if og == 0 else 4
        gstat, comb = z.gstat, z.comb
        s_g = ExitStack()
        p_g = _dbuf(s_g, nc, "p_g%d" % og, 2, [128, NE], F32, psum=True)
        p_ct = _dbuf(s_g, nc, "p_ct%d" % og, 2, [128, 128], F32, psum=True)
        if True:
            for tl in range(4):
                oc = og * 4 + tl
                TE.wait_ge(c_tpe.h, 16 * (og + 1))
                if pm_eff < 4:
                    continue
                if oc >= 2 and (oc - 2) in z.pg_free:
                    TE.wait_ge(z.c_gs2.h, z.pg_free[oc - 2])
                for cc in range(16):
                    mm = TE.matmul(p_g[oc % 2][:], xg[gp][:, cc, tl * 128:(tl + 1) * 128],
                                   z.gw_b[:, cc, :], start=(cc == 0), stop=(cc == 15))
                c_gmm.inc(mm)
                G = gstat
                c_v, c_s = z.c_gv2, z.c_gs2

                def vstep(fn):
                    c_s.wait(VE)
                    c_v.inc(fn())

                def sstep(fn):
                    c_v.wait(SC)
                    c_s.inc(fn())
                c_gmm.wait(VE)
                vstep(lambda: VE.reduce_max(G[:, 0:1], p_g[oc % 2][:], AX.X,
                                            negate=True))
                sstep(lambda: SC.activation(G[:, 2:10], p_g[oc % 2][:], AF.Exp,
                                            bias=G[:, 0:1]))
                z.pg_free[oc] = c_s.n
                vstep(lambda: VE.reduce_max(G[:, 10:11], G[:, 2:10], AX.X))
                sstep(lambda: SC.activation(G[:, 11:12], G[:, 10:11], AF.Copy))
                vstep(lambda: VE.tensor_scalar(G[:, 12:20], G[:, 2:10], G[:, 11:12],
                                               None, ALU.is_ge))
                sstep(lambda: SC.activation(G[:, 20:28], G[:, 12:20], AF.Copy))
                vstep(lambda: VE.scalar_tensor_tensor(G[:, 28:36], G[:, 20:28], -4.0,
                                                      G[:, 2:10], ALU.mult, ALU.add))
                sstep(lambda: SC.activation(G[:, 36:44], G[:, 28:36], AF.Copy))
                vstep(lambda: VE.reduce_max(G[:, 44:45], G[:, 36:44], AX.X))
                sstep(lambda: SC.activation(G[:, 45:46], G[:, 44:45], AF.Copy))
                sstep(lambda: SC.activation(G[:, 46:47], G[:, 44:45], AF.Ln,
                                            bias=G[:, 11:12]))
                sstep(lambda: SC.activation(G[:, 47:48], G[:, 46:47], AF.Exp,
                                            scale=-1.0))
                vstep(lambda: VE.tensor_scalar(G[:, 48:56], G[:, 2:10], G[:, 45:46],
                                               None, ALU.is_ge))
                sstep(lambda: SC.activation(G[:, 56:64], G[:, 48:56], AF.Copy))
                vstep(lambda: VE.tensor_mul(G[:, 64:72], G[:, 56:64], G[:, 2:10]))
                sstep(lambda: SC.activation(G[:, 72:80], G[:, 64:72], AF.Copy))
                vstep(lambda: VE.tensor_scalar(G[:, 80:88], G[:, 72:80], G[:, 47:48],
                                               None, ALU.mult))
                sstep(lambda: SC.activation(G[:, 88:96], G[:, 80:88], AF.Copy))
                c_s.wait(VE)
                c_gc.inc(VE.tensor_mul(comb[:, oc, :], G[:, 88:96], z.alpha_bc[:]))
                if pm_eff < 5:
                    continue
                if oc >= 1:
                    c_ctm.wait(VE)
                c_gc.inc(VE.tensor_mul(z.comb32[:, 0:NE], G[:, 88:96],
                                       z.alpha_bc[:]))
                if pm_eff < 6:
                    continue
                c_gc.wait(TE)
                if tl >= 2:
                    z.c_gct.wait(TE, z.c_gct.n - 1)
                c_ctm.inc(TE.matmul(p_ct[tl % 2][:], z.comb32[:], z.idb[:],
                                    start=True, stop=True))
                if pm_eff < 7:
                    continue
                c_ctm.wait(VE)
                z.c_gct.inc(VE.tensor_copy(
                    z.combT[0:NE, oc * 128:(oc + 1) * 128],
                    p_ct[tl % 2][0:NE, :]))
        s_g.close()
        if z.pa_mode >= 4:
            SY.wait_ge(c_tpe.h, 16 * (og + 1))
            z.c_xod.inc(SY.dma_start(
                out=z.xnod.ap()[:, :, og * 512:(og + 1) * 512].rearrange("c p t -> p c t"),
                in_=xg[gp][:]), 16)

    def qkv_group(fg):
        grp = NG_OWN + fg
        gp = grp % 2
        t0 = fg * 512
        with ExitStack() as sqv:
            p_qk = _dbuf(sqv, nc, "p_qk%d" % fg, HPC, [128, 512], F32, psum=True)
            p_v = _dbuf(sqv, nc, "p_v%d" % fg, 2, [128, HPC * 128], F32, psum=True)
            for proj in range(2):
                wbf = (z.wq_bf, z.wk_bf)[proj]
                wsb = (wqs, wks)[proj]
                for cc in range(16):
                    gi = fg * 32 + proj * 16 + cc
                    if gi >= 2:
                        SY.wait_ge(c_wqf.h, gi - 1)
                    c_wql.inc(SY.dma_start(out=wsb[gi % 2][:], in_=wbf.ap()[cc]))
                    c_wql.wait_done(TE, gi)
                    TE.wait_ge(c_tpe.h, 16 * (grp + 1))
                    if cc == 0:
                        TE.wait_ge(c_qev.h, c_qev.n)
                    for h in range(HPC):
                        mm = TE.matmul(p_qk[h][:], wsb[gi % 2][:, h * 128:(h + 1) * 128],
                                       xg[gp][:, cc, :], start=(cc == 0), stop=(cc == 15))
                    c_wqf.inc(mm)
                SC.wait_ge(c_wqf.h, fg * 32 + proj * 16 + 16)
                for h in range(HPC):
                    ev = SC.activation(
                        ((QT, KT)[proj])[:, h, t0:t0 + 512], p_qk[h][:], AF.Identity,
                        bias=(z.bqs if proj == 0 else z.bkT)[:, h:h + 1],
                        scale=(SCALE if proj == 0 else 1.0))
                c_qev.inc(ev)
            for tlp in range(2):
                vi = fg * 2 + tlp
                for cc in range(16):
                    gvi = vi * 16 + cc
                    if gvi >= 2:
                        SY.wait_ge(c_wvf.h, gvi - 1)
                    c_wvl.inc(SY.dma_start(out=wvs[gvi % 2][:], in_=z.wv_bf.ap()[cc]))
                    c_wvl.wait_done(TE, gvi)
                    if cc == 0:
                        TE.wait_ge(c_vev.h, c_vev.n)
                    for j in range(2):
                        tl = tlp * 2 + j
                        mm = TE.matmul(p_v[j][:], xg[gp][:, cc, tl * 128:(tl + 1) * 128],
                                       wvs[gvi % 2][:], start=(cc == 0), stop=(cc == 15))
                    c_wvf.inc(mm)
                VE.wait_ge(c_wvf.h, 16 * (vi + 1))
                for j in range(2):
                    kcg = fg * 4 + tlp * 2 + j
                    ev = VE.tensor_add(
                        Vsb[:, :, kcg, :],
                        p_v[j][:].rearrange("p (h d) -> p h d", d=128),
                        z.bv_bc[:].rearrange("p (h d) -> p h d", d=128))
                c_vev.inc(ev)

    pm = z.pa_mode
    for og in range(NG_OWN):
        for tl in range(4):
            ln_tile(og * 4 + tl)
            if pm >= 2:
                transpose_tile(og * 4 + tl)
        if pm >= 3:
            gating_group(og)
    for fg in range(NG_FULL):
        grp = NG_OWN + fg
        for tl in range(4):
            ln_tile(grp * 4 + tl)
            if pm >= 2:
                transpose_tile(grp * 4 + tl)
        if pm >= 9:
            qkv_group(fg)


def _pb(z):
    nc, SY, TE, VE, SC, GP = z.nc, z.SY, z.TE, z.VE, z.SC, z.GP
    mkc = z.mkc
    QT, KT, Vsb, yT = z.QT, z.KT, z.Vsb, z.yT
    s = ExitStack()
    sb = lambda name, shape, dt: s.enter_context(nc.sbuf_tensor(name, shape, dt))
    pt = _dbuf(s, nc, "s_pt", 4, [128, 1024], BF16)
    cmask = sb("s_cmsk", [128, 4, 512], BF16)
    lb_sb = sb("s_lb", [128, 512], F32)
    linv = sb("s_linv", [1, 512], F32)
    linv_b = sb("s_linvb", [1, 512], BF16)
    p_s = _dbuf(s, nc, "p_s", 2, [128, 1024], F32, psum=True)
    p_o = s.enter_context(nc.psum_tensor("p_o", [128, 512], F32))
    p_l = s.enter_context(nc.psum_tensor("p_l", [1, 512], F32))
    p_lb = s.enter_context(nc.psum_tensor("p_lb", [128, 512], F32))

    cmask_f = sb("s_cmskf", [128, 4, 512], F32)
    c_cm = mkc("c_cm")
    c_cm.inc(SY.dma_start(out=cmask_f[:],
                          in_=z.cmask_in.ap().rearrange("p (o q) -> p o q", o=4)), 16)
    c_cm.wait(VE)
    c_cm.inc(VE.tensor_copy(cmask[:], cmask_f[:]))
    c_smm = mkc("c_smm")
    c_exp = mkc("c_exp")
    c_msk = mkc("c_msk")
    c_avp = mkc("c_avp")
    c_li = mkc("c_li")
    c_lbm = mkc("c_lbm")
    c_yev = mkc("c_yev")
    for h in range(HPC):
        for qt in range(8):
            q0 = qt * 512
            nkc = (qt + 1) * 4
            npair = nkc // 2
            meta = []

            def emit_avl(entry):
                pb, pi, diag, exp_cnt, mask_cnt = entry
                TE.wait_ge(c_exp.h, exp_cnt)
                if diag:
                    TE.wait_ge(c_msk.h, mask_cnt)
                mm = None
                for j in range(2):
                    kc = 2 * pi + j
                    TE.matmul(p_o[:], Vsb[:, h, kc, :], pt[pb][:, j * 512:(j + 1) * 512],
                              start=(kc == 0), stop=(kc == nkc - 1))
                    mm = TE.matmul(p_l[:], z.ones_col[:], pt[pb][:, j * 512:(j + 1) * 512],
                                   start=(kc == 0), stop=(kc == nkc - 1))
                c_avp.inc(mm)

            for pi in range(npair):
                P0 = c_smm.n
                if P0 >= 2:
                    TE.wait_ge(c_exp.h, P0 - 1)
                for j in range(2):
                    kc = 2 * pi + j
                    mm = TE.matmul(p_s[P0 % 2][:, j * 512:(j + 1) * 512],
                                   KT[:, h, kc * 128:(kc + 1) * 128],
                                   QT[:, h, q0:q0 + 512], start=True, stop=True)
                c_smm.inc(mm)
                SC.wait_ge(c_smm.h, c_smm.n)
                pb = P0 % 4
                if P0 >= 4:
                    SC.wait_ge(c_avp.h, P0 - 3)
                c_exp.inc(SC.activation(pt[pb][:], p_s[P0 % 2][:], AF.Exp))
                diag = (2 * pi) >= nkc - 4
                if diag:
                    VE.wait_ge(c_exp.h, c_exp.n)
                    for j in range(2):
                        o = (2 * pi + j) - (nkc - 4)
                        mk = VE.tensor_mul(pt[pb][:, j * 512:(j + 1) * 512],
                                           pt[pb][:, j * 512:(j + 1) * 512],
                                           cmask[:, o, :])
                    c_msk.inc(mk)
                meta.append((pb, pi, diag, c_exp.n, c_msk.n))
                if len(meta) >= 2:
                    emit_avl(meta[-2])
            emit_avl(meta[-1])

            c_avp.wait(SC)
            SC.activation(linv[:], p_l[:], AF.Ln)
            c_li.inc(SC.activation(linv_b[:], linv[:], AF.Exp, scale=-1.0))
            c_li.wait(TE)
            c_lbm.inc(TE.matmul(p_lb[:], z.ones_rowb[:, 0:128], linv_b[:],
                                start=True, stop=True))
            c_lbm.wait(SC)
            c_li.inc(SC.activation(lb_sb[:], p_lb[:], AF.Copy))
            c_li.wait(VE)
            c_yev.inc(VE.tensor_mul(yT[:, h, q0:q0 + 512], p_o[:], lb_sb[:]))
            TE.wait_ge(c_yev.h, c_yev.n)

    c_agin = mkc("c_agin")
    c_yev.wait(SY)
    c_agin.inc(SY.dma_start(out=z.ag_in.ap().rearrange("(h d) t -> d h t", d=128),
                            in_=yT[:]), 16)
    z.c_ag = mkc("c_ag")
    GP.wait_ge(c_agin.h, 16)
    z.c_ag.inc(GP.collective_compute(
        "AllGather", ALU.bypass, replica_groups=GROUPS,
        ins=[z.ag_in.ap().opt()], outs=[z.ag_out.ap().opt()]), 1)
    s.close()


def _pc(z, s):
    nc, SY, TE, VE, SC, GP = z.nc, z.SY, z.TE, z.VE, z.SC, z.GP
    mkc = z.mkc
    ffn_acc = z.ffn_acc
    sb = lambda name, shape, dt: s.enter_context(nc.sbuf_tensor(name, shape, dt))
    xno = sb("s_xno", [128, 16, TOK], BF16)
    hc_sb = sb("s_hc", [128, 16, TOK], BF16)
    bcast = sb("s_bcast", [128, NE, TOK], BF16)
    sel8 = sb("s_sel8", [NE, NE * 128], BF16)
    sel8_f = sb("s_sel8f", [NE, NE * 128], F32)
    f1s = _dbuf(s, nc, "s_f1s", 2, [128, 1024], F32)
    f1b = _dbuf(s, nc, "s_f1b", 2, [128, 16, 128], BF16)
    f2s = _dbuf(s, nc, "s_f2s", 2, [128, 1024], F32)
    f2b = _dbuf(s, nc, "s_f2b", 2, [128, 16, 128], BF16)
    gel = _dbuf(s, nc, "s_gel", 2, [128, 512], F32)

    c_xno = mkc("c_xno")
    z.c_xod.wait(SY)
    c_xno.inc(SY.dma_start(out=xno[:], in_=z.xnod.ap().rearrange("c p t -> p c t")), 16)
    c_sel = mkc("c_sel")
    c_sel.inc(SY.dma_start(out=sel8_f[:], in_=z.sel8_in.ap()[:]), 16)
    c_sel.wait(VE)
    c_sel.inc(VE.tensor_copy(sel8[:], sel8_f[:]))
    c_ct1 = mkc("c_ct1")
    for thv in range(2):
        c_ct1.inc(SY.dma_start(out=z.combT[NE:NE + 1, thv * 512:(thv + 1) * 512],
                               in_=z.ones_rowb[:]), 16)

    c_bcm = mkc("c_bcm")
    c_bce = mkc("c_bce")
    with nc.psum_tensor("p_fb_0", [128, 512], F32) as p_fb0, \
         nc.psum_tensor("p_fb_1", [128, 512], F32) as p_fb1:
        p_fb = (p_fb0, p_fb1)
        z.c_gct.wait(TE)
        c_sel.wait(TE)
        c_ct1.wait(TE)
        for e in range(NE):
            for th in range(2):
                i = e * 2 + th
                if i >= 2:
                    TE.wait_ge(c_bce.h, i - 1)
                c_bcm.inc(TE.matmul(p_fb[i % 2][:], sel8[:, e * 128:(e + 1) * 128],
                                    z.combT[0:NE, th * 512:(th + 1) * 512],
                                    start=True, stop=True))
                c_bcm.wait(VE)
                c_bce.inc(VE.tensor_copy(bcast[:, e, th * 512:(th + 1) * 512], p_fb[i % 2][:]))
        for mc in range(16):
            for th in range(2):
                i = 16 + mc * 2 + th
                TE.wait_ge(c_bce.h, i - 1)
                c_bcm.inc(TE.matmul(p_fb[i % 2][:], z.f2b_b[:, mc * 128:(mc + 1) * 128],
                                    z.combT[:, th * 512:(th + 1) * 512],
                                    start=True, stop=True))
                c_bcm.wait(VE)
                c_bce.inc(VE.tensor_copy(ffn_acc[:, mc, th * 512:(th + 1) * 512],
                                         p_fb[i % 2][:]))

    z.c_f2ev = mkc("c_f2ev")
    c_f2ev = z.c_f2ev
    c_f1i = CtrP(st_of(z), nc, "c_f1i")
    c_f1c = mkc("c_f1c")
    c_f1f = mkc("c_f1f")
    c_gelc = mkc("c_gelc")
    c_hcw = mkc("c_hcw")
    c_f2i = CtrP(st_of(z), nc, "c_f2i")
    c_f2c = mkc("c_f2c")
    c_f2f = mkc("c_f2f")

    p_h = _dbuf(s, nc, "p_h", 2, [128, 1024], F32, psum=True)
    p_f2 = _dbuf(s, nc, "p_f2", 2, [128, 1024], F32, psum=True)
    for e in range(NE):
        for ic in range(16):
            gi = e * 16 + ic
            for hf in range(2):
                sj = gi * 2 + hf
                if sj >= 2:
                    SY.wait_ge(c_f1c.h, sj - 1)
                c_f1i.inc(SY.dma_start(
                    out=f1s[sj % 2][:].rearrange("p (c i) -> p c i", c=8),
                    in_=z.fc1w.ap()[e, hf * 1024:(hf + 1) * 1024, ic * 128:(ic + 1) * 128]
                    .rearrange("(c p) i -> p c i", p=128)))
                c_f1i.wait_done(GP, sj)
                if hf == 0 and gi >= 2:
                    GP.wait_ge(c_f1f.h, gi - 1)
                c_f1c.inc(GP.tensor_copy(
                    f1b[gi % 2][:, hf * 8:(hf + 1) * 8, :].rearrange("p c i -> p (c i)"),
                    f1s[sj % 2][:]))
            TE.wait_ge(c_f1c.h, 2 * (gi + 1))
            if gi == 0:
                TE.wait_ge(c_xno.h, 16)
                TE.wait_ge(c_bce.h, c_bce.n)
            if gi >= 2:
                TE.wait_ge(c_gelc.h, 2 * (gi - 1))
            for cc in range(16):
                for th in range(2):
                    mm = TE.matmul(p_h[gi % 2][:, th * 512:(th + 1) * 512],
                                   f1b[gi % 2][:, cc, :],
                                   xno[:, cc, th * 512:(th + 1) * 512],
                                   start=(cc == 0), stop=(cc == 15))
            c_f1f.inc(mm)
            SC.wait_ge(c_f1f.h, gi + 1)
            if gi >= 1:
                SC.wait_ge(c_hcw.h, gi)
            for th in range(2):
                c_gelc.inc(SC.activation(gel[th][:], p_h[gi % 2][:, th * 512:(th + 1) * 512],
                                         AF.Gelu, bias=z.f1bt[:, ic * NE + e:ic * NE + e + 1]))
            VE.wait_ge(c_gelc.h, 2 * (gi + 1))
            if e > 0 and ic == 0:
                VE.wait_ge(c_f2f.h, 16 * e)
            for th in range(2):
                w = VE.tensor_mul(hc_sb[:, ic, th * 512:(th + 1) * 512], gel[th][:],
                                  bcast[:, e, th * 512:(th + 1) * 512])
            c_hcw.inc(w)
        for mc in range(16):
            gi = e * 16 + mc
            for hf in range(2):
                sj = gi * 2 + hf
                if sj >= 2:
                    SC.wait_ge(c_f2c.h, sj - 1)
                c_f2i.inc(SC.dma_start(
                    out=f2s[sj % 2][:].rearrange("p (c i) -> p c i", c=8),
                    in_=z.fc2w.ap()[e, hf * 1024:(hf + 1) * 1024, mc * 128:(mc + 1) * 128]
                    .rearrange("(c p) i -> p c i", p=128)))
                c_f2i.wait_done(GP, sj)
                if hf == 0 and gi >= 2:
                    GP.wait_ge(c_f2f.h, gi - 1)
                c_f2c.inc(GP.tensor_copy(
                    f2b[gi % 2][:, hf * 8:(hf + 1) * 8, :].rearrange("p c i -> p (c i)"),
                    f2s[sj % 2][:]))
            TE.wait_ge(c_f2c.h, 2 * (gi + 1))
            TE.wait_ge(c_hcw.h, 16 * (e + 1))
            if gi >= 2:
                TE.wait_ge(c_f2ev.h, gi - 1)
            for ic in range(16):
                for th in range(2):
                    mm = TE.matmul(p_f2[gi % 2][:, th * 512:(th + 1) * 512],
                                   f2b[gi % 2][:, ic, :],
                                   hc_sb[:, ic, th * 512:(th + 1) * 512],
                                   start=(ic == 0), stop=(ic == 15))
            c_f2f.inc(mm)
            VE.wait_ge(c_f2f.h, gi + 1)
            c_f2ev.inc(VE.tensor_add(ffn_acc[:, mc, :], ffn_acc[:, mc, :], p_f2[gi % 2][:]))


def _pd(z, s):
    nc, SY, TE, VE, SC, GP = z.nc, z.SY, z.TE, z.VE, z.SC, z.GP
    mkc = z.mkc
    sb = lambda name, shape, dt: s.enter_context(nc.sbuf_tensor(name, shape, dt))
    yg = sb("s_yg", [128, 16, TOK], BF16)
    cand = _dbuf(s, nc, "s_cand", 2, [128, T], BF16)
    ygt = sb("s_ygt", [128, TOK], BF16)
    wos = _dbuf(s, nc, "s_wos", 2, [128, 512], BF16)
    ost = _dbuf(s, nc, "s_ost", 2, [128, 512], F32)
    p_t = _dbuf(s, nc, "p_t", 8, [128, 512], F32, psum=True)

    z.c_ad = CtrP(st_of(z), nc, "c_ad")
    c_ad = z.c_ad
    c_cand = CtrP(st_of(z), nc, "c_cand")
    c_ygw = mkc("c_ygw")
    c_wosl = CtrP(st_of(z), nc, "c_wosl")
    c_wosf = mkc("c_wosf")
    c_ost = mkc("c_ost")

    SY.wait_ge(z.c_ag.h, 1)
    for hcx in range(16):
        p = hcx % 2
        if hcx >= 2:
            SY.wait_ge(c_ygw.h, hcx - 1)
        c_cand.inc(SY.dma_start(out=cand[p][:],
                                in_=z.ag_out.ap()[hcx * 128:(hcx + 1) * 128, :]))
        c_cand.wait_done(VE, hcx)
        VE.tensor_scalar(yg[:, hcx, :], cand[p][:, 0:TOK], z.msel[:, 0:1], None, ALU.mult)
        w = None
        for j in range(1, 4):
            VE.tensor_scalar(ygt[:], cand[p][:, j * TOK:(j + 1) * TOK],
                             z.msel[:, j:j + 1], None, ALU.mult)
            w = VE.tensor_add(yg[:, hcx, :], yg[:, hcx, :], ygt[:])
        c_ygw.inc(w)
    for mg in range(4):
        for hcx in range(16):
            gi = mg * 16 + hcx
            p = gi % 2
            if gi >= 2:
                SC.wait_ge(c_wosf.h, gi - 1)
            c_wosl.inc(SC.dma_start(out=wos[p][:],
                                    in_=z.wo_bf.ap()[hcx][:, mg * 512:(mg + 1) * 512]))
            c_wosl.wait_done(TE, gi)
            if mg == 0:
                TE.wait_ge(c_ygw.h, hcx + 1)
            if hcx == 0 and mg > 0:
                TE.wait_ge(c_ost.h, 8 * mg)
            for tc in range(8):
                mm = TE.matmul(p_t[tc][:], yg[:, hcx, tc * 128:(tc + 1) * 128], wos[p][:],
                               start=(hcx == 0), stop=(hcx == 15))
            c_wosf.inc(mm)
        for tc in range(8):
            oidx = mg * 8 + tc
            VE.wait_ge(c_wosf.h, 16 * (mg + 1))
            if oidx >= 2:
                c_ad.wait_done(VE, oidx - 2)
            c_ost.inc(VE.tensor_copy(ost[oidx % 2][:], p_t[tc][:]))
            SY.wait_ge(c_ost.h, c_ost.n)
            c_ad.inc(SY.dma_start(
                out=z.attn_d.ap()[tc * 128:(tc + 1) * 128, mg * 512:(mg + 1) * 512],
                in_=ost[oidx % 2][:]))


def _pe2(z, st):
    nc, SY, TE, VE, SC, GP = z.nc, z.SY, z.TE, z.VE, z.SC, z.GP
    mkc = z.mkc
    s = ExitStack()
    sb = lambda name, shape, dt: s.enter_context(nc.sbuf_tensor(name, shape, dt))
    arow = _dbuf(s, nc, "s_arow", 2, [128, C], F32)
    xrow = _dbuf(s, nc, "s_xrow", 2, [128, C], F32)
    osb = _dbuf(s, nc, "s_osb", 2, [128, C], F32)
    p_f = _dbuf(s, nc, "p_f", 2, [128, C], F32, psum=True)
    c_ar = CtrP(st_of(z), nc, "c_ar")
    c_xr = CtrP(st_of(z), nc, "c_xr")
    c_ftr = mkc("c_ftr")
    c_ofin = mkc("c_ofin")
    c_out = mkc("c_out")
    z.c_ad.wait_all(SY)
    for tc in range(8):
        p = tc % 2
        if tc >= 2:
            SY.wait_ge(c_ofin.h, tc - 1)
        c_ar.inc(SY.dma_start(out=arow[p][:],
                              in_=z.attn_d.ap()[tc * 128:(tc + 1) * 128, :]))
        c_xr.inc(SY.dma_start(out=xrow[p][:],
                              in_=z.x_cat.ap()[tc * 128:(tc + 1) * 128, :]))
        if tc >= 2:
            TE.wait_ge(c_ofin.h, tc - 1)
        for mc in range(16):
            tr = TE.transpose(p_f[p][:, mc * 128:(mc + 1) * 128],
                              z.ffn_acc[:, mc, tc * 128:(tc + 1) * 128], z.idf[:])
        c_ftr.inc(tr)
        VE.wait_ge(c_ftr.h, tc + 1)
        c_ar.wait_done(VE, tc)
        c_xr.wait_done(VE, tc)
        VE.scalar_tensor_tensor(osb[p][:], p_f[p][:], 1.0, xrow[p][:], ALU.mult, ALU.add)
        c_ofin.inc(VE.tensor_add(osb[p][:], osb[p][:], arow[p][:]))
        SY.wait_ge(c_ofin.h, tc + 1)
        c_out.inc(SY.dma_start(out=z.out_own.ap()[tc * 128:(tc + 1) * 128, :],
                               in_=osb[p][:]), 16)
    SY.wait_ge(c_out.h, 128)
    s.close()


_NC_CACHE = {}


def _get_nc():
    if "nc" not in _NC_CACHE:
        _NC_CACHE["nc"] = build_program()
    return _NC_CACHE["nc"]


def kernel(L0_attn_x, L0_attn_Wq, L0_attn_bq, L0_attn_Wk, L0_attn_bk, L0_attn_Wv,
           L0_attn_bv, L0_attn_Wo, L0_attn_bo, L0_ffn_gate_w, L0_ffn_fc1_w,
           L0_ffn_fc1_b, L0_ffn_fc2_w, L0_ffn_fc2_b, L0_ffn_alpha):
    f32 = np.float32
    bf16 = ml_dtypes.bfloat16
    x = np.ascontiguousarray(np.asarray(L0_attn_x, dtype=f32))
    Wq = np.asarray(L0_attn_Wq, f32)
    Wk = np.asarray(L0_attn_Wk, f32)
    Wv = np.asarray(L0_attn_Wv, f32)
    bqv = np.asarray(L0_attn_bq, f32)
    bkv = np.asarray(L0_attn_bk, f32)
    bvv = np.asarray(L0_attn_bv, f32)
    Wo = np.ascontiguousarray(np.asarray(L0_attn_Wo, f32))
    bo = np.asarray(L0_attn_bo, f32)
    gw = np.ascontiguousarray(np.asarray(L0_ffn_gate_w, f32))
    f1w = np.ascontiguousarray(np.asarray(L0_ffn_fc1_w, f32))
    f1b = np.asarray(L0_ffn_fc1_b, f32)
    f2w = np.ascontiguousarray(np.asarray(L0_ffn_fc2_w, f32))
    f2b = np.asarray(L0_ffn_fc2_b, f32)
    alpha = np.asarray(L0_ffn_alpha, f32)

    idb = np.eye(128, dtype=f32)
    idf = np.eye(128, dtype=f32)
    ones_col = np.ones((128, 1), dtype=f32)
    ones_rowf = np.ones((1, 128), dtype=f32)
    ones_rowb = np.ones((1, 512), dtype=f32)
    sel8 = np.kron(np.eye(NE, dtype=f32), np.ones((1, 128), f32))
    kp = np.arange(128)[:, None]
    qq = np.arange(512)[None, :]
    cmask = np.zeros((128, 4, 512), dtype=f32)
    for o in range(4):
        cmask[:, o, :] = (qq >= o * 128 + kp).astype(f32)
    cmask = np.ascontiguousarray(cmask.reshape(128, 4 * 512))
    fc1bt = np.ascontiguousarray(f1b.T)
    fc2be = np.ascontiguousarray(np.concatenate([f2b, bo.reshape(1, -1)], axis=0))

    in_maps = []
    for c in range(8):
        b, g = c // 4, c % 4
        x_catv = np.ascontiguousarray(
            np.concatenate([x[b, TOK * g:TOK * (g + 1)], x[b]], axis=0))
        hs = slice(4 * g, 4 * g + 4)
        mselv = np.zeros((128, 4), dtype=f32)
        mselv[:, g] = 1.0
        in_maps.append({
            "x_cat": x_catv,
            "wq": np.ascontiguousarray(np.concatenate(list(Wq[hs]), axis=1)),
            "wk": np.ascontiguousarray(np.concatenate(list(Wk[hs]), axis=1)),
            "wv": np.ascontiguousarray(np.concatenate(list(Wv[hs]), axis=1)),
            "bq": np.ascontiguousarray(bqv[hs].reshape(-1)),
            "bk": np.ascontiguousarray(bkv[hs].reshape(-1)),
            "bv_row": np.ascontiguousarray(bvv[hs].reshape(1, -1)),
            "wo": Wo,
            "gate_w": gw, "fc1w": f1w, "fc1bt": fc1bt,
            "fc2w": f2w, "fc2be": fc2be,
            "alpha_row": np.ascontiguousarray(alpha.reshape(1, -1)),
            "msel": mselv,
            "idb": idb, "idf": idf, "ones_col": ones_col,
            "ones_rowf": ones_rowf, "ones_rowb": ones_rowb,
            "sel8": sel8, "cmask": cmask,
        })

    nc = _get_nc()
    res = run_bass_kernel_spmd(nc, in_maps, core_ids=list(range(8)))
    _NC_CACHE["res"] = res
    out = np.empty((B, T, C), dtype=np.float32)
    for c in range(8):
        b, g = c // 4, c % 4
        out[b, TOK * g:TOK * (g + 1)] = res.results[c]["out_own"]
    return out
